# version 15 (final)
# baseline (speedup 1.0000x reference)
import numpy as np
import ml_dtypes
import concourse.bass as bass
import concourse.mybir as mybir
from concourse.bass_utils import run_bass_kernel_spmd
from contextlib import ExitStack

F32 = mybir.dt.float32
BF16 = mybir.dt.bfloat16
AF = mybir.ActivationFunctionType
ALU = mybir.AluOpType
AX = mybir.AxisListType

D = 2048
DC = 16
TQ = 1024
TK = 1536
NT = 8
H = 16
KV = 4
HD = 128
SUB = 64
E = 8192
EC = 64
SLAB = 640
WIN = 512
EPS_LN = 1e-5
EPS_DIFF = 1e-6
EPS_Q = 1e-8
NELEM_W = 8192 * 2048

STAGE = "full"
class _SkipRest(Exception):
    pass


_ORDER = {"ln1": 0, "v": 1, "kk": 1, "qkv": 1, "att": 2, "att1": 2, "wo": 3,
          "wo0": 3, "wo1": 3, "ln2": 3, "x1": 3, "full": 4, "quant": 0,
          "x1+q": 3, "att+q": 2, "v+q": 1, "x1+q1": 3, "x1+q2": 3, "x1+q3": 3, "x1+q4": 3, "x1+q5": 3}

_nc_cache = {}


def build_nc(stage=STAGE):
    nc = bass.Bass("TRN2", target_bir_lowering=False)

    xkv = nc.dram_tensor("xkv", [D, TK], F32, kind="ExternalInput").ap()
    wq = nc.dram_tensor("wq", [D, D], F32, kind="ExternalInput").ap()
    wk = nc.dram_tensor("wk", [D, 512], F32, kind="ExternalInput").ap()
    wv = nc.dram_tensor("wv", [D, 512], F32, kind="ExternalInput").ap()
    wo = nc.dram_tensor("wo", [D, D], F32, kind="ExternalInput").ap()
    wup_s = nc.dram_tensor("wup_s", [1024, D], F32, kind="ExternalInput").ap()
    wg_s = nc.dram_tensor("wg_s", [1024, D], F32, kind="ExternalInput").ap()
    wd_s = nc.dram_tensor("wd_s", [256, E], F32, kind="ExternalInput").ap()
    masks_in = nc.dram_tensor("masks", [128, NT, SLAB], BF16, kind="ExternalInput").ap()
    lamb_in = nc.dram_tensor("lamb", [128, H], F32, kind="ExternalInput").ap()
    ident_in = nc.dram_tensor("ident", [128, 128], F32, kind="ExternalInput").ap()
    cosq_in = nc.dram_tensor("cosq", [SUB, TQ], F32, kind="ExternalInput").ap()
    sinq_in = nc.dram_tensor("sinq", [SUB, TQ], F32, kind="ExternalInput").ap()
    cosk_in = nc.dram_tensor("cosk", [SUB, TK], F32, kind="ExternalInput").ap()
    sink_in = nc.dram_tensor("sink", [SUB, TK], F32, kind="ExternalInput").ap()
    out = nc.dram_tensor("out", [D, TQ], F32, kind="ExternalOutput").ap()
    dbg = nc.dram_tensor("dbg", [16, TQ], F32, kind="ExternalOutput").ap()

    sc_upT = nc.dram_tensor("sc_upT", [128, 8, 2048], BF16, kind="Internal").ap()
    sc_gT = nc.dram_tensor("sc_gT", [128, 8, 2048], BF16, kind="Internal").ap()
    sc_dT = nc.dram_tensor("sc_dT", [128, 8, 2048], BF16, kind="Internal").ap()
    g_upT = nc.dram_tensor("g_upT", [8 * 128, 8, 2048], BF16, kind="Internal", addr_space="Shared").ap()
    g_gT = nc.dram_tensor("g_gT", [8 * 128, 8, 2048], BF16, kind="Internal", addr_space="Shared").ap()
    g_dT = nc.dram_tensor("g_dT", [8 * 128, 8, 2048], BF16, kind="Internal", addr_space="Shared").ap()
    prt_in = nc.dram_tensor("prt_in", [1, 4], F32, kind="Internal").ap()
    prt_out = nc.dram_tensor("prt_out", [1, 4], F32, kind="Internal", addr_space="Shared").ap()
    x2scr = nc.dram_tensor("x2scr", [D, TQ], F32, kind="Internal").ap()

    root = ExitStack()
    nc._kernel_ctx = root

    def sbuf(name, shape, dtype, ctx):
        return ctx.enter_context(nc.sbuf_tensor(name, shape, dtype))

    def psum(name, shape, dtype, ctx):
        return ctx.enter_context(nc.psum_tensor(name, shape, dtype))

    lamb = sbuf("lamb_sb", [128, H], F32, root)
    neglam = sbuf("neglam_sb", [128, H], F32, root)
    ident = sbuf("ident_sb", [128, 128], BF16, root)
    ones1 = sbuf("ones1", [128, 1], BF16, root)
    ones1f = sbuf("ones1f", [1, 128], F32, root)
    onescol = sbuf("onescol", [128, 1], F32, root)
    hrow = sbuf("hrow", [1, 4], F32, root)
    h_bc = sbuf("h_bc", [128, 4], F32, root)
    negh_bc = sbuf("negh_bc", [128, 4], F32, root)
    partials = sbuf("partials", [1, 4], F32, root)
    c_row = sbuf("c_row", [1, 2], F32, root)
    eps1 = sbuf("eps1", [1, 1], F32, root)
    identf = sbuf("identf", [128, 128], F32, root)
    h2 = sbuf("h2", [128, DC, TQ], BF16, root)

    s_ld = root.enter_context(nc.semaphore("s_ld"))
    s_ld2 = root.enter_context(nc.semaphore("s_ld2"))
    s_ldg = root.enter_context(nc.semaphore("s_ldg"))
    s_cc = root.enter_context(nc.semaphore("s_cc"))
    s_st = root.enter_context(nc.semaphore("s_st"))
    s_a = root.enter_context(nc.semaphore("s_a"))
    s_b = root.enter_context(nc.semaphore("s_b"))
    s_c = root.enter_context(nc.semaphore("s_c"))
    s_d = root.enter_context(nc.semaphore("s_d"))
    s_e = root.enter_context(nc.semaphore("s_e"))
    s_f = root.enter_context(nc.semaphore("s_f"))
    s_g = root.enter_context(nc.semaphore("s_g"))
    s_h = root.enter_context(nc.semaphore("s_h"))
    s_i = root.enter_context(nc.semaphore("s_i"))
    s_j = root.enter_context(nc.semaphore("s_j"))
    jn = [0]

    def jedge(eng, instr):
        instr.then_inc(s_j, 1)
        jn[0] += 1
        eng.wait_ge(s_j, jn[0])

    tot = {k: 0 for k in "abcdefghi"}
    tot.update(ld=0, ld2=0, ldg=0, st=0, cc=0)
    lvl = _ORDER[stage]
    with_quant = (stage in ("full", "quant") or stage.endswith("+q")
                  or stage.endswith("+q1") or stage.endswith("+q2"))
    no_cc = any(stage.endswith(s) for s in ("+q1", "+q3", "+q4", "+q5"))
    no_emit = not stage.endswith("+q") and not stage.endswith("+q1") and with_quant and stage != "full" and stage != "quant"
    loads_only = stage.endswith("+q3")
    no_mms = stage.endswith("+q4")
    no_reduce = stage.endswith("+q5")

    for suf in ("+q1", "+q2", "+q3", "+q4", "+q5", "+q"):
        if stage.endswith(suf):
            stage = stage[:-len(suf)]
            break

    def dbg_out_bf16(name, srcs):
        st0 = tot["st"]
        with nc.Block(name) as blk:
            @blk.gpsimd
            def _(g):
                for dc, ap in enumerate(srcs):
                    g.dma_start(out=out[128 * dc:128 * (dc + 1), :], in_=ap).then_inc(s_st, 16)
                g.wait_ge(s_st, st0 + 16 * len(srcs))
        tot["st"] += 16 * len(srcs)

    with nc.Block("p0") as blk:
        ld0, ldg0 = tot["ld"], tot["ldg"]

        @blk.sync
        def _(sync):
            sync.dma_start(out=lamb[:], in_=lamb_in).then_inc(s_ld, 16)
            sync.dma_start(out=identf[:], in_=ident_in).then_inc(s_ld, 16)

        @blk.vector
        def _(v):
            v.memset(ones1[:], 1.0)
            v.memset(ones1f[:], 1.0)
            v.memset(onescol[:], 1.0)
            v.memset(eps1[:], EPS_LN)
            v.memset(partials[:], 0.0)
            v.wait_ge(s_ld, ld0 + 32)
            v.tensor_scalar(neglam[:], lamb[:], -1.0, None, op0=ALU.mult)
            v.tensor_copy(ident[:], identf[:])
    tot["ld"] += 32

    if with_quant:
        def piece_ap(mi, i):
            if mi == 0:
                return wup_s[128 * i:128 * (i + 1), :]
            if mi == 1:
                return wg_s[128 * i:128 * (i + 1), :]
            dc, seg = divmod(i, 4)
            return wd_s[128 * dc:128 * (dc + 1), 2048 * seg:2048 * (seg + 1)]

        with ExitStack() as qctx:
            wf = [sbuf(f"q_wf{i}", [128, 2048], F32, qctx) for i in range(2)]
            absb = [sbuf(f"q_ab{i}", [128, 2048], F32, qctx) for i in range(2)]
            ps_scale = psum("ps_scale", [1, 512], F32, qctx)
            scrow = sbuf("q_scrow", [1, 512], F32, qctx)

            for mi in range(3):
                ld0, a0, b0 = tot["ld"], tot["a"], tot["b"]
                with nc.Block(f"p1_{mi}") as blk:
                    @blk.sync
                    def _(sync, mi=mi, ld0=ld0, a0=a0):
                        for i in range(8):
                            if i >= 2:
                                sync.wait_ge(s_a, a0 + i - 1)
                            sync.dma_start(out=wf[i % 2][:], in_=piece_ap(mi, i)).then_inc(s_ld, 16)

                    @blk.scalar
                    def _(s, ld0=ld0, b0=b0):
                        for i in range(8):
                            s.wait_ge(s_ld, ld0 + 16 * (i + 1))
                            if i >= 2:
                                s.wait_ge(s_b, b0 + i - 1)
                            s.activation(absb[i % 2][:], wf[i % 2][:], AF.Abs).then_inc(s_a, 1)

                    @blk.tensor
                    def _(t, mi=mi, a0=a0):
                        for i in range(8):
                            t.wait_ge(s_a, a0 + i + 1)
                            for k in range(4):
                                mm = t.matmul(ps_scale[:], onescol[:], absb[i % 2][:, 512 * k:512 * (k + 1)],
                                              start=(i == 0 and k == 0), stop=(i == 7 and k == 3))
                            mm.then_inc(s_b, 1)
                tot["ld"] += 16 * 8
                tot["a"] += 8
                tot["b"] += 8
                with nc.Block(f"p1r_{mi}") as blk:
                    @blk.scalar
                    def _(s, mi=mi):
                        s.copy(scrow[:], ps_scale[:])
                with nc.Block(f"p1r2_{mi}") as blk:
                    @blk.vector
                    def _(v, mi=mi):
                        w = 256
                        while w >= 1:
                            jedge(v, v.tensor_add(scrow[0:1, 0:w], scrow[0:1, 0:w],
                                                  scrow[0:1, w:2 * w]))
                            w //= 2
                        v.tensor_copy(partials[0:1, mi:mi + 1], scrow[0:1, 0:1])

            with nc.Block("p1_cc") as blk:
                @blk.sync
                def _(sync):
                    sync.dma_start(out=prt_in, in_=partials[:]).then_inc(s_ld, 16)
                tot["ld"] += 16

                @blk.gpsimd
                def _(g):
                    g.wait_ge(s_ld, tot["ld"])
                    g.collective_compute("AllReduce", ALU.add, replica_groups=[list(range(8))],
                                         ins=[prt_in.opt()], outs=[prt_out.opt()]).then_inc(s_cc, 1)
                    g.wait_ge(s_cc, tot["cc"] + 1)
                    g.dma_start(out=partials[:], in_=prt_out).then_inc(s_ldg, 16)
                tot["cc"] += 1
                tot["ldg"] += 16

            with nc.Block("p1_rows") as blk:
                @blk.vector
                def _(v):
                    v.wait_ge(s_ldg, tot["ldg"])
                    v.tensor_scalar(hrow[:], partials[:], 0.5 / NELEM_W, 0.5 * EPS_Q,
                                    op0=ALU.mult, op1=ALU.add)
            with nc.Block("p1_rows2") as blk:
                @blk.vector
                def _(v):
                    v.tensor_scalar(c_row[0:1, 0:1], hrow[0:1, 2:3], 1.0, None, op0=ALU.mult)
                    v.tensor_tensor(c_row[0:1, 1:2], hrow[0:1, 2:3], hrow[0:1, 2:3], op=ALU.mult)

            with ExitStack() as bctx:
                ps_hbc = psum("ps_hbc", [128, 4], F32, bctx)
                with nc.Block("p1_bc") as blk:
                    @blk.tensor
                    def _(t):
                        t.matmul(ps_hbc[:], ones1f[:], hrow[:], start=True, stop=True)
                with nc.Block("p1_bce") as blk:
                    @blk.scalar
                    def _(s):
                        s.copy(h_bc[:], ps_hbc[:])
                with nc.Block("p1_bcn") as blk:
                    @blk.vector
                    def _(v):
                        v.tensor_scalar(negh_bc[:], h_bc[:], -1.0, None, op0=ALU.mult)

            sg1 = [sbuf(f"q_sg1{i}", [128, 2048], BF16, qctx) for i in range(2)]
            sg2b = [sbuf(f"q_sg2{i}", [128, 2048], BF16, qctx) for i in range(2)]
            t2b = [sbuf(f"q_t2{i}", [128, 2048], BF16, qctx) for i in range(2)]
            tT_all = sbuf("q_tT", [128, 8, 2048], BF16, qctx)
            ps_tr = [psum(f"ps_tr{i}", [128, 2048], BF16, qctx) for i in range(2)]

            for mi in range(3):
                ld0, a0, b0, c0, d0 = tot["ld"], tot["a"], tot["b"], tot["c"], tot["d"]
                st0 = tot["st"]
                with nc.Block(f"p2_{mi}") as blk:
                    @blk.sync
                    def _(sync, mi=mi, ld0=ld0, a0=a0):
                        for i in range(8):
                            if i >= 2:
                                sync.wait_ge(s_a, a0 + i - 1)
                            sync.dma_start(out=wf[i % 2][:], in_=piece_ap(mi, i)).then_inc(s_ld, 16)

                    @blk.scalar
                    def _(s, mi=mi, ld0=ld0, b0=b0):
                        for i in range(8):
                            s.wait_ge(s_ld, ld0 + 16 * (i + 1))
                            if i >= 2:
                                s.wait_ge(s_b, b0 + i - 1)
                            s.activation(sg1[i % 2][:], wf[i % 2][:], AF.Sign,
                                         bias=negh_bc[:, mi:mi + 1])
                            s.activation(sg2b[i % 2][:], wf[i % 2][:], AF.Sign,
                                         bias=h_bc[:, mi:mi + 1]).then_inc(s_a, 1)

                    @blk.gpsimd
                    def _(g, a0=a0, c0=c0):
                        for i in range(8):
                            g.wait_ge(s_a, a0 + i + 1)
                            if i >= 2:
                                g.wait_ge(s_c, c0 + i - 1)
                            g.tensor_add(t2b[i % 2][:], sg1[i % 2][:], sg2b[i % 2][:]).then_inc(s_b, 1)

                    @blk.tensor
                    def _(t, b0=b0, d0=d0):
                        for i in range(8):
                            t.wait_ge(s_b, b0 + i + 1)
                            if i >= 2:
                                t.wait_ge(s_d, d0 + i - 1)
                            for dc in range(16):
                                mm = t.transpose(ps_tr[i % 2][:, 128 * dc:128 * (dc + 1)],
                                                 t2b[i % 2][:, 128 * dc:128 * (dc + 1)], ident[:])
                            mm.then_inc(s_c, 1)

                    @blk.vector
                    def _(v, c0=c0):
                        for i in range(8):
                            v.wait_ge(s_c, c0 + i + 1)
                            v.tensor_copy(tT_all[:, i, :], ps_tr[i % 2][:]).then_inc(s_d, 1)
                tot["ld"] += 16 * 8
                tot["a"] += 8
                tot["b"] += 8
                tot["c"] += 8
                tot["d"] += 8

                with nc.Block(f"p2s_{mi}") as blk:
                    @blk.sync
                    def _(sync, mi=mi):
                        dst = (sc_upT, sc_gT, sc_dT)[mi]
                        sync.dma_start(out=dst, in_=tT_all[:]).then_inc(s_st, 16)
                tot["st"] += 16

            with nc.Block("p2_cc") as blk:
                @blk.gpsimd
                def _(g):
                    g.wait_ge(s_st, tot["st"])
                    g.collective_compute("AllGather", ALU.bypass, replica_groups=[list(range(8))],
                                         ins=[sc_upT.opt()], outs=[g_upT.opt()]).then_inc(s_cc, 1)
                    g.collective_compute("AllGather", ALU.bypass, replica_groups=[list(range(8))],
                                         ins=[sc_gT.opt()], outs=[g_gT.opt()]).then_inc(s_cc, 1)
                    g.collective_compute("AllGather", ALU.bypass, replica_groups=[list(range(8))],
                                         ins=[sc_dT.opt()], outs=[g_dT.opt()]).then_inc(s_cc, 1)
                    g.wait_ge(s_cc, tot["cc"] + 3)
                tot["cc"] += 3
        cc_gathers_done = tot["cc"]
    else:
        cc_gathers_done = 0

    if with_quant:
        with ExitStack() as tctx:
            tsc = sbuf("tbl_scratch", [1, 16], F32, tctx)
            with nc.Block("tbl_settle") as blk:
                @blk.vector
                def _(v):
                    v.memset(tsc[:], 1.0)
            with nc.Block("tbl_settle2") as blk:
                @blk.scalar
                def _(s):
                    s.square(tsc[0:1, 0:4], tsc[0:1, 0:4])
                    s.activation(tsc[0:1, 4:8], tsc[0:1, 4:8], AF.Sqrt)
                    s.activation(tsc[0:1, 8:12], tsc[0:1, 8:12], AF.Exp, scale=-1.0)
                    s.activation(tsc[0:1, 12:16], tsc[0:1, 12:16], AF.Silu)

    def emit_ln_rows(name, srx_, srxx_, marow_, arow_, brow_, n, c_ap=None, c2_ap=None):
        def vblock(tag, f):
            with nc.Block(name + tag) as blk:
                blk.vector(f)
        vblock("_ra", lambda v: v.tensor_scalar(marow_[:], srx_[:], 1.0 / n, None, op0=ALU.mult))
        vblock("_rb", lambda v: v.tensor_scalar(arow_[:], srxx_[:], 1.0 / n, None, op0=ALU.mult))
        vblock("_rc", lambda v: v.tensor_tensor(brow_[:], marow_[:], marow_[:], op=ALU.mult))
        vblock("_rd", lambda v: v.tensor_sub(arow_[:], arow_[:], brow_[:]))
        if c2_ap is not None:
            vblock("_re", lambda v: v.tensor_scalar(arow_[:], arow_[:], c2_ap, None, op0=ALU.mult))
        with nc.Block(name + "_r2") as blk:
            @blk.scalar
            def _(s):
                s.activation(arow_[:], arow_[:], AF.Sqrt, bias=eps1[0:1, 0:1], scale=1.0)
        vblock("_rf", lambda v: v.reciprocal(arow_[:], arow_[:]))
        if c_ap is not None:
            vblock("_rg", lambda v: v.tensor_scalar(arow_[:], arow_[:], c_ap, None, op0=ALU.mult))
        vblock("_rh", lambda v: v.tensor_mul(brow_[:], marow_[:], arow_[:]))

    def emit_bcast(name, arow_, brow_, a_s_, b_s_, ps_, width, off=0):
        nblk = width // 512
        with nc.Block(name + "_m") as blk:
            @blk.tensor
            def _(t):
                for k in range(nblk):
                    t.matmul(ps_[:, 512 * k:512 * (k + 1)], ones1f[:],
                             arow_[0:1, off + 512 * k:off + 512 * (k + 1)],
                             start=True, stop=True)
        with nc.Block(name + "_e") as blk:
            @blk.scalar
            def _(s):
                s.copy(a_s_[:, 0:width], ps_[:, 0:width])
        with nc.Block(name + "_m2") as blk:
            @blk.tensor
            def _(t):
                for k in range(nblk):
                    t.matmul(ps_[:, 512 * k:512 * (k + 1)], ones1f[:],
                             brow_[0:1, off + 512 * k:off + 512 * (k + 1)],
                             start=True, stop=True)
        with nc.Block(name + "_e2") as blk:
            @blk.scalar
            def _(s):
                s.copy(b_s_[:, 0:width], ps_[:, 0:width])

    with ExitStack() as attn_ctx:
        av = sbuf("av", [128, H, TQ], BF16, attn_ctx)
        qkctx = ExitStack()
        qrot = sbuf("qrot", [128, H, TQ], BF16, qkctx)
        krot = sbuf("krot", [128, KV, TK], BF16, qkctx)
        vtok = sbuf("vtok", [128, 12, 512], BF16, qkctx)

        with ExitStack() as pctx:
            h1 = sbuf("h1", [128, DC, TK], BF16, pctx)

            with ExitStack() as lctx:
                xf = [sbuf(f"l1_xf{i}", [128, 512], F32, lctx) for i in range(2)]
                xb = [sbuf(f"l1_xb{i}", [128, 512], BF16, lctx) for i in range(2)]
                xq = [sbuf(f"l1_xq{i}", [128, 512], BF16, lctx) for i in range(2)]
                srx = psum("l1_srx", [1, TK], F32, lctx)
                srxx = psum("l1_srxx", [1, TK], F32, lctx)
                marow = sbuf("l1_ma", [1, TK], F32, lctx)
                arow = sbuf("l1_a", [1, TK], F32, lctx)
                brow = sbuf("l1_b", [1, TK], F32, lctx)
                ps_bc = psum("l1_bc", [128, 512], F32, lctx)
                a_s = sbuf("l1_as", [128, 512], F32, lctx)
                b_s = sbuf("l1_bs", [128, 512], F32, lctx)
                tmp_s = [sbuf(f"l1_tmp{i}", [128, 512], F32, lctx) for i in range(2)]

                niter = 3 * DC
                ld0, a0, b0, c0 = tot["ld"], tot["a"], tot["b"], tot["c"]
                with nc.Block("p3_stats") as blk:
                    @blk.sync
                    def _(sync):
                        it = 0
                        for th in range(3):
                            for dc in range(DC):
                                if it >= 2:
                                    sync.wait_ge(s_a, a0 + it - 1)
                                    sync.wait_ge(s_b, b0 + it - 1)
                                sync.dma_start(
                                    out=xf[it % 2][:],
                                    in_=xkv[128 * dc:128 * (dc + 1), 512 * th:512 * (th + 1)]
                                ).then_inc(s_ld, 16)
                                it += 1

                    @blk.vector
                    def _(v):
                        for it in range(niter):
                            v.wait_ge(s_ld, ld0 + 16 * (it + 1))
                            if it >= 2:
                                v.wait_ge(s_c, c0 + it - 1)
                            v.tensor_copy(xb[it % 2][:], xf[it % 2][:]).then_inc(s_a, 1)

                    @blk.scalar
                    def _(s):
                        for it in range(niter):
                            s.wait_ge(s_ld, ld0 + 16 * (it + 1))
                            if it >= 2:
                                s.wait_ge(s_c, c0 + it - 1)
                            s.square(xq[it % 2][:], xf[it % 2][:]).then_inc(s_b, 1)

                    @blk.tensor
                    def _(t):
                        it = 0
                        for th in range(3):
                            for dc in range(DC):
                                t.wait_ge(s_a, a0 + it + 1)
                                t.wait_ge(s_b, b0 + it + 1)
                                t.matmul(srx[0:1, 512 * th:512 * (th + 1)], ones1[:], xb[it % 2][:],
                                         start=(dc == 0), stop=(dc == DC - 1))
                                t.matmul(srxx[0:1, 512 * th:512 * (th + 1)], ones1[:], xq[it % 2][:],
                                         start=(dc == 0), stop=(dc == DC - 1)).then_inc(s_c, 1)
                                it += 1
                tot["ld"] += 16 * niter
                tot["a"] += niter
                tot["b"] += niter
                tot["c"] += niter

                emit_ln_rows("p3", srx, srxx, marow, arow, brow, D)

                for th in range(3):
                    emit_bcast(f"p3b{th}", arow, brow, a_s, b_s, ps_bc, 512, off=512 * th)
                    ld0, a0, b0 = tot["ld"], tot["a"], tot["b"]
                    with nc.Block(f"p3_app{th}") as blk:
                        @blk.sync
                        def _(sync, th=th, ld0=ld0, a0=a0):
                            for dc in range(DC):
                                if dc >= 2:
                                    sync.wait_ge(s_a, a0 + dc - 1)
                                sync.dma_start(
                                    out=xf[dc % 2][:],
                                    in_=xkv[128 * dc:128 * (dc + 1), 512 * th:512 * (th + 1)]
                                ).then_inc(s_ld, 16)

                        @blk.gpsimd
                        def _(g, ld0=ld0, b0=b0):
                            for dc in range(DC):
                                g.wait_ge(s_ld, ld0 + 16 * (dc + 1))
                                if dc >= 2:
                                    g.wait_ge(s_b, b0 + dc - 1)
                                g.tensor_mul(tmp_s[dc % 2][:], xf[dc % 2][:], a_s[:]).then_inc(s_a, 1)

                        @blk.vector
                        def _(v, th=th, a0=a0):
                            for dc in range(DC):
                                v.wait_ge(s_a, a0 + dc + 1)
                                v.tensor_sub(h1[:, dc, 512 * th:512 * (th + 1)],
                                             tmp_s[dc % 2][:], b_s[:]).then_inc(s_b, 1)
                    tot["ld"] += 16 * DC
                    tot["a"] += DC
                    tot["b"] += DC

            if stage == "ln1":
                dbg_out_bf16("dbg_ln1", [h1[:, dc, 512:1536] for dc in range(DC)])
            if lvl >= 1:
              with ExitStack() as vctx:
                  wv_sb = sbuf("wv_sb", [128, DC, 512], BF16, vctx)
                  wvf = [sbuf(f"wvf{i}", [128, DC, 128], F32, vctx) for i in range(2)]
                  vps = [psum(f"v_ps{i}", [128, 512], F32, vctx) for i in range(2)]
                  ld0v, a0, b0, f0v = tot["ld"], tot["a"], tot["b"], tot["f"]
                  wv_r = wv.rearrange("(dc p) m -> p dc m", p=128)
                  with nc.Block("p4_v") as blk:
                      @blk.sync
                      def _(sync):
                          for i in range(4):
                              if i >= 2:
                                  sync.wait_ge(s_f, f0v + i - 1)
                              sync.dma_start(out=wvf[i % 2][:],
                                             in_=wv_r[:, :, 128 * i:128 * (i + 1)]).then_inc(s_ld, 16)

                      @blk.vector
                      def _(v):
                          for i in range(4):
                              v.wait_ge(s_ld, ld0v + 16 * (i + 1))
                              v.tensor_copy(wv_sb.rearrange("p dc (a b) -> p dc a b", b=128)[:, :, i, :],
                                            wvf[i % 2][:]).then_inc(s_f, 1)

                      @blk.tensor
                      def _(t):
                          t.wait_ge(s_f, f0v + 4)
                          for jt in range(12):
                              if jt >= 2:
                                  t.wait_ge(s_a, a0 + jt - 1)
                              for dc in range(DC):
                                  mm = t.matmul(vps[jt % 2][:], h1[:, dc, 128 * jt:128 * (jt + 1)],
                                                wv_sb[:, dc, :], start=(dc == 0), stop=(dc == DC - 1))
                              mm.then_inc(s_b, 1)

                      @blk.scalar
                      def _(s):
                          for jt in range(12):
                              s.wait_ge(s_b, b0 + jt + 1)
                              s.copy(vtok[:, jt, :], vps[jt % 2][:]).then_inc(s_a, 1)
                  tot["ld"] += 64
                  tot["f"] += 4
                  tot["a"] += 12
                  tot["b"] += 12

            if stage == "v":
                with nc.Block("dbg_v") as blk:
                    st0v = tot["st"]
                    @blk.gpsimd
                    def _(g):
                        for j in range(12):
                            g.dma_start(out=out[128 * (j % 8):128 * (j % 8) + 128,
                                                512 * (j // 8):512 * (j // 8) + 512],
                                        in_=vtok[:, j, :]).then_inc(s_st, 16)
                        g.wait_ge(s_st, st0v + 16 * 12)
                tot["st"] += 16 * 12
            if lvl >= 1 and stage != "v":
              with ExitStack() as kctx:
                  wkb = sbuf("wkb", [128, DC, 128], BF16, kctx)
                  wkf = sbuf("wkf", [128, DC, 128], F32, kctx)
                  kps = [psum(f"k_ps{i}", [128, TK], F32, kctx) for i in range(2)]
                  ck_full = sbuf("ck_full", [128, TK], F32, kctx)
                  sk_full = sbuf("sk_full", [128, TK], F32, kctx)
                  kf = sbuf("kf", [128, TK], F32, kctx)
                  kswap = sbuf("kswap", [128, TK], F32, kctx)

                  ld0, ldg0, ld20k = tot["ld"], tot["ldg"], tot["ld2"]
                  a0, b0, c0, d0, e0, f0, g0k = (tot["a"], tot["b"], tot["c"], tot["d"],
                                                 tot["e"], tot["f"], tot["g"])
                  wk_r = wk.rearrange("(dc p) m -> p dc m", p=128)
                  with nc.Block("p4_k") as blk:
                      @blk.sync
                      def _(sync):
                          sync.dma_start(out=ck_full[0:SUB, :], in_=cosk_in).then_inc(s_ld, 16)
                          sync.dma_start(out=ck_full[SUB:128, :], in_=cosk_in).then_inc(s_ld, 16)
                          sync.dma_start(out=sk_full[SUB:128, :], in_=sink_in).then_inc(s_ld, 16)
                          sync.dma_start(out=sk_full[0:SUB, :], in_=sink_in).then_inc(s_ld, 16)
                          for kc in range(KV):
                              sync.wait_ge(s_a, a0 + kc + 1)
                              sync.dma_start(out=kswap[0:SUB, :], in_=kf[SUB:128, :]).then_inc(s_ld, 16)
                              sync.dma_start(out=kswap[SUB:128, :], in_=kf[0:SUB, :]).then_inc(s_ld, 16)


                      @blk.gpsimd
                      def _(g):
                          g.wait_ge(s_f, f0 + 1)
                          for kc in range(KV):
                              g.wait_ge(s_ld2, ld20k + 16 * (kc + 1))
                              if kc >= 1:
                                  g.wait_ge(s_b, b0 + kc)
                              g.tensor_copy(wkb[:], wkf[:]).then_inc(s_g, 1)
                              if kc >= 1:
                                  g.wait_ge(s_ld, ld0 + 64 + 32 * kc)
                                  g.tensor_mul(kswap[:], kswap[:], sk_full[:]).then_inc(s_d, 1)
                          g.wait_ge(s_ld, ld0 + 64 + 32 * KV)
                          g.tensor_mul(kswap[:], kswap[:], sk_full[:]).then_inc(s_d, 1)

                      @blk.vector
                      def _(v):
                          v.wait_ge(s_ld, ld0 + 64)
                          v.tensor_scalar(sk_full[0:SUB, :], sk_full[0:SUB, :], -1.0, None,
                                          op0=ALU.mult).then_inc(s_f, 1)
                          for kc in range(KV):
                              v.wait_ge(s_a, a0 + kc + 1)
                              v.wait_ge(s_ld, ld0 + 64 + 32 * (kc + 1))
                              v.tensor_mul(kf[:], kf[:], ck_full[:])
                              v.wait_ge(s_d, d0 + kc + 1)
                              v.tensor_add(krot[:, kc, :], kf[:], kswap[:]).then_inc(s_e, 1)

                      @blk.tensor
                      def _(t):
                          for kc in range(KV):
                              t.wait_ge(s_g, g0k + kc + 1)
                              if kc >= 2:
                                  t.wait_ge(s_a, a0 + kc - 1)
                              for dc in range(DC):
                                  for nb in range(3):
                                      mm = t.matmul(kps[kc % 2][:, 512 * nb:512 * (nb + 1)],
                                                    wkb[:, dc, :],
                                                    h1[:, dc, 512 * nb:512 * (nb + 1)],
                                                    start=(dc == 0), stop=(dc == DC - 1))
                              mm.then_inc(s_b, 1)

                      @blk.scalar
                      def _(s):
                          for kc in range(KV):
                              if kc >= 1:
                                  s.wait_ge(s_g, g0k + kc)
                              s.dma_start(out=wkf[:],
                                          in_=wk_r[:, :, 128 * kc:128 * (kc + 1)]).then_inc(s_ld2, 16)
                              if kc >= 1:
                                  s.wait_ge(s_b, b0 + kc)
                                  s.wait_ge(s_e, e0 + kc - 1) if kc >= 2 else None
                                  s.copy(kf[:], kps[(kc - 1) % 2][:]).then_inc(s_a, 1)
                          s.wait_ge(s_b, b0 + KV)
                          s.wait_ge(s_e, e0 + KV - 1)
                          s.copy(kf[:], kps[(KV - 1) % 2][:]).then_inc(s_a, 1)
                  tot["ld"] += 64 + 32 * KV
                  tot["ld2"] += 16 * KV
                  tot["g"] += KV
                  tot["a"] += KV
                  tot["b"] += KV
                  tot["d"] += KV
                  tot["e"] += KV
                  tot["f"] += 1

            if stage == "kk":
                with nc.Block("dbg_kk") as blk:
                    st0k = tot["st"]
                    @blk.gpsimd
                    def _(g):
                        for gg in range(4):
                            g.dma_start(out=out[128 * gg:128 * gg + 128, :],
                                        in_=krot[:, gg, 0:1024]).then_inc(s_st, 16)
                        g.wait_ge(s_st, st0k + 16 * 4)
                tot["st"] += 16 * 4
            if lvl >= 1 and stage not in ("v", "kk"):
              with ExitStack() as qqctx:
                  wqb = [sbuf(f"wqb{i}", [128, DC, 128], BF16, qqctx) for i in range(2)]
                  wqf = sbuf("wqf", [128, DC, 128], F32, qqctx)
                  qps = [psum(f"q_ps{i}", [128, TQ], F32, qqctx) for i in range(2)]
                  cq_full = sbuf("cq_full", [128, TQ], F32, qqctx)
                  sq_full = sbuf("sq_full", [128, TQ], F32, qqctx)
                  qf = sbuf("qf", [128, TQ], F32, qqctx)
                  qswap = sbuf("qswap", [128, TQ], F32, qqctx)

                  ld0, ldg0, ld20q = tot["ld"], tot["ldg"], tot["ld2"]
                  a0, b0, d0, e0, f0, g0q = (tot["a"], tot["b"], tot["d"], tot["e"],
                                             tot["f"], tot["g"])
                  wq_r = wq.rearrange("(dc p) m -> p dc m", p=128)
                  with nc.Block("p4_q") as blk:
                      @blk.sync
                      def _(sync):
                          sync.dma_start(out=cq_full[0:SUB, :], in_=cosq_in).then_inc(s_ld, 16)
                          sync.dma_start(out=cq_full[SUB:128, :], in_=cosq_in).then_inc(s_ld, 16)
                          sync.dma_start(out=sq_full[SUB:128, :], in_=sinq_in).then_inc(s_ld, 16)
                          sync.dma_start(out=sq_full[0:SUB, :], in_=sinq_in).then_inc(s_ld, 16)
                          for hc in range(H):
                              sync.wait_ge(s_a, a0 + hc + 1)
                              sync.dma_start(out=qswap[0:SUB, :], in_=qf[SUB:128, :]).then_inc(s_ld, 16)
                              sync.dma_start(out=qswap[SUB:128, :], in_=qf[0:SUB, :]).then_inc(s_ld, 16)


                      @blk.gpsimd
                      def _(g):
                          g.wait_ge(s_f, f0 + 1)
                          for hc in range(H):
                              g.wait_ge(s_ld2, ld20q + 16 * (hc + 1))
                              if hc >= 2:
                                  g.wait_ge(s_b, b0 + hc - 1)
                              g.tensor_copy(wqb[hc % 2][:], wqf[:]).then_inc(s_g, 1)
                              if hc >= 1:
                                  g.wait_ge(s_ld, ld0 + 64 + 32 * hc)
                                  g.tensor_mul(qswap[:], qswap[:], sq_full[:]).then_inc(s_d, 1)
                          g.wait_ge(s_ld, ld0 + 64 + 32 * H)
                          g.tensor_mul(qswap[:], qswap[:], sq_full[:]).then_inc(s_d, 1)

                      @blk.vector
                      def _(v):
                          v.wait_ge(s_ld, ld0 + 64)
                          v.tensor_scalar(sq_full[0:SUB, :], sq_full[0:SUB, :], -1.0, None,
                                          op0=ALU.mult).then_inc(s_f, 1)
                          for hc in range(H):
                              v.wait_ge(s_a, a0 + hc + 1)
                              v.wait_ge(s_ld, ld0 + 64 + 32 * (hc + 1))
                              v.tensor_mul(qf[:], qf[:], cq_full[:])
                              v.wait_ge(s_d, d0 + hc + 1)
                              v.tensor_add(qrot[:, hc, :], qf[:], qswap[:]).then_inc(s_e, 1)

                      @blk.tensor
                      def _(t):
                          for hc in range(H):
                              t.wait_ge(s_g, g0q + hc + 1)
                              t.wait_ge(s_f, f0 + 1)
                              if hc >= 2:
                                  t.wait_ge(s_a, a0 + hc - 1)
                              for dc in range(DC):
                                  for nb in range(2):
                                      mm = t.matmul(qps[hc % 2][:, 512 * nb:512 * (nb + 1)],
                                                    wqb[hc % 2][:, dc, :],
                                                    h1[:, dc, 512 + 512 * nb:1024 + 512 * nb],
                                                    start=(dc == 0), stop=(dc == DC - 1))
                              mm.then_inc(s_b, 1)

                      @blk.scalar
                      def _(s):
                          for hc in range(H):
                              if hc >= 1:
                                  s.wait_ge(s_g, g0q + hc)
                              s.dma_start(out=wqf[:],
                                          in_=wq_r[:, :, 128 * hc:128 * (hc + 1)]).then_inc(s_ld2, 16)
                              if hc >= 1:
                                  s.wait_ge(s_b, b0 + hc)
                                  s.wait_ge(s_e, e0 + hc - 1) if hc >= 2 else None
                                  s.copy(qf[:], qps[(hc - 1) % 2][:]).then_inc(s_a, 1)
                          s.wait_ge(s_b, b0 + H)
                          s.wait_ge(s_e, e0 + H - 1)
                          s.copy(qf[:], qps[(H - 1) % 2][:]).then_inc(s_a, 1)
                  tot["ld"] += 64 + 32 * H
                  tot["ld2"] += 16 * H
                  tot["g"] += H
                  tot["a"] += H
                  tot["b"] += H
                  tot["d"] += H
                  tot["e"] += H
                  tot["f"] += 1

        if stage == "qkv":
            dbg_out_bf16("dbg_qkv", [qrot[:, hc, :] for hc in range(H)])
        if lvl >= 2:
          with ExitStack() as actx:
              masks = sbuf("masks_sb", [128, NT, SLAB], BF16, actx)
              a1ps = psum("a1ps", [128, SLAB], F32, actx)
              a2ps = psum("a2ps", [128, SLAB], F32, actx)
              trps = psum("trps", [128, SLAB], BF16, actx)
              avps = psum("avps", [128, 128], F32, actx)
              e1 = sbuf("e1", [128, SLAB], F32, actx)
              e2 = sbuf("e2", [128, SLAB], F32, actx)
              em1 = sbuf("em1", [128, SLAB], F32, actx)
              em2 = sbuf("em2", [128, SLAB], F32, actx)
              dnf = sbuf("dnf", [128, SLAB], F32, actx)
              dnb = sbuf("dnb", [128, SLAB], BF16, actx)
              dT = sbuf("dT", [128, SLAB], BF16, actx)
              nm1 = sbuf("nm1", [128, 1], F32, actx)
              nm2 = sbuf("nm2", [128, 1], F32, actx)
              ss1 = sbuf("ss1", [128, 1], F32, actx)
              ss2 = sbuf("ss2", [128, 1], F32, actx)
              uacc = sbuf("uacc", [128, 1], F32, actx)
              betan = sbuf("betan", [128, 1], F32, actx)
              recd = sbuf("recd", [128, 1], F32, actx)

              iters = [(hh, tt) for hh in range(H) for tt in range(NT)]
              if stage == "att1":
                  iters = [(0, 0)]
              NI = len(iters)
              ld0 = tot["ld"]
              a0, b0, c0, d0, e0, f0, g0, h0, i0 = (tot[k] for k in "abcdefghi")

              with nc.Block("p5") as blk:
                  @blk.sync
                  def _(sync):
                      sync.dma_start(out=masks[:], in_=masks_in).then_inc(s_ld, 16)

                  @blk.tensor
                  def _(t):
                      def tr_av(j):
                          hp, tp = iters[j]
                          gp = hp // 4
                          t.wait_ge(s_e, e0 + j + 1)
                          if j >= 1:
                              t.wait_ge(s_g, g0 + j)
                          for ch in range(5):
                              mm = t.transpose(trps[:, 128 * ch:128 * (ch + 1)],
                                               dnb[:, 128 * ch:128 * (ch + 1)], ident[:])
                          mm.then_inc(s_f, 1)
                          t.wait_ge(s_g, g0 + j + 1)
                          if j >= 1:
                              t.wait_ge(s_i, i0 + j)
                          for ch in range(5):
                              mm = t.matmul(avps[:], vtok[:, tp + ch, 128 * gp:128 * (gp + 1)],
                                            dT[:, 128 * ch:128 * (ch + 1)],
                                            start=(ch == 0), stop=(ch == 4))
                          mm.then_inc(s_h, 1)

                      for ii, (hh, tt) in enumerate(iters):
                          gg = hh // 4
                          if ii >= 1:
                              t.wait_ge(s_c, c0 + ii)
                          sl = 128 * tt
                          t.matmul(a1ps[:, 0:512], qrot[0:SUB, hh, sl:sl + 128],
                                   krot[0:SUB, gg, sl:sl + 512], start=True, stop=True)
                          t.matmul(a2ps[:, 0:512], qrot[SUB:128, hh, sl:sl + 128],
                                   krot[SUB:128, gg, sl:sl + 512], start=True, stop=True)
                          t.matmul(a1ps[:, 512:640], qrot[0:SUB, hh, sl:sl + 128],
                                   krot[0:SUB, gg, sl + 512:sl + 640], start=True, stop=True)
                          t.matmul(a2ps[:, 512:640], qrot[SUB:128, hh, sl:sl + 128],
                                   krot[SUB:128, gg, sl + 512:sl + 640], start=True, stop=True).then_inc(s_a, 1)
                          if ii >= 1:
                              tr_av(ii - 1)
                      tr_av(NI - 1)

                  @blk.vector
                  def _(v):
                      v.wait_ge(s_ld, ld0 + 16)
                      for ii, (hh, tt) in enumerate(iters):
                          v.wait_ge(s_a, a0 + ii + 1)
                          if ii >= 1:
                              v.wait_ge(s_c, c0 + ii)
                          v.reduce_max(nm1[:], a1ps[:], axis=AX.X, negate=True)
                          v.reduce_max(nm2[:], a2ps[:], axis=AX.X, negate=True).then_inc(s_b, 1)
                          v.wait_ge(s_c, c0 + ii + 1)
                          v.scalar_tensor_tensor(em1[:], e1[:], 1.0, masks[:, tt, :],
                                                 op0=ALU.mult, op1=ALU.mult, accum_out=ss1[:])
                          em2i = v.scalar_tensor_tensor(em2[:], e2[:], 1.0, masks[:, tt, :],
                                                        op0=ALU.mult, op1=ALU.mult,
                                                        accum_out=ss2[:])
                          em2i.then_inc(s_d, 1)
                          v.wait_ge(s_d, d0 + ii + 1)
                          if ii >= 1:
                              v.wait_ge(s_f, f0 + ii)
                          jedge(v, v.reciprocal(betan[:], ss2[:]))
                          jedge(v, v.scalar_tensor_tensor(betan[:], betan[:],
                                                          neglam[:, hh:hh + 1], ss1[:],
                                                          op0=ALU.mult, op1=ALU.mult))
                          v.scalar_tensor_tensor(dnf[:], em2[:], betan[:], em1[:],
                                                 op0=ALU.mult, op1=ALU.add)
                          jedge(v, v.tensor_scalar(dnf[:], dnf[:], 0.0, 0.0, op0=ALU.max,
                                                   op1=ALU.add, accum_out=uacc[:]))
                          jedge(v, v.scalar_tensor_tensor(recd[:], ss1[:], EPS_DIFF, uacc[:],
                                                          op0=ALU.mult, op1=ALU.add))
                          jedge(v, v.reciprocal(recd[:], recd[:]))
                          v.tensor_scalar(dnb[:], dnf[:], recd[:], None, op0=ALU.mult).then_inc(s_e, 1)

                  @blk.scalar
                  def _(s):
                      def evacs(j):
                          s.wait_ge(s_f, f0 + j + 1)
                          s.copy(dT[:], trps[:]).then_inc(s_g, 1)
                          s.wait_ge(s_h, h0 + j + 1)
                          hp, tp = iters[j]
                          s.copy(av[:, hp, 128 * tp:128 * (tp + 1)], avps[:]).then_inc(s_i, 1)

                      for ii, (hh, tt) in enumerate(iters):
                          s.wait_ge(s_b, b0 + ii + 1)
                          if ii >= 1:
                              s.wait_ge(s_d, d0 + ii)
                          s.activation(e1[:], a1ps[:], AF.Exp, bias=nm1[:], scale=1.0)
                          s.activation(e2[:], a2ps[:], AF.Exp, bias=nm2[:], scale=1.0).then_inc(s_c, 1)
                          if ii >= 1:
                              evacs(ii - 1)
                      evacs(NI - 1)

              tot["ld"] += 16
              for k in "abcde":
                  tot[k] += NI
              for k in "fghi":
                  tot[k] += NI

              if stage == "att1":
                  st0a = tot["st"]
                  with nc.Block("dbg_att1") as blk:
                      @blk.gpsimd
                      def _(g):
                          g.dma_start(out=out[0:128, 0:SLAB], in_=e1[:]).then_inc(s_st, 16)
                          g.dma_start(out=out[128:256, 0:SLAB], in_=em1[:]).then_inc(s_st, 16)
                          g.dma_start(out=out[256:384, 0:SLAB], in_=dnb[:]).then_inc(s_st, 16)
                          g.dma_start(out=out[384:512, 0:SLAB], in_=dT[:]).then_inc(s_st, 16)
                          g.dma_start(out=out[512:640, 0:SLAB], in_=em2[:]).then_inc(s_st, 16)
                          g.dma_start(out=out[768:896, 0:1024], in_=av[:, 0, :]).then_inc(s_st, 16)

                      @blk.sync
                      def _(sync):
                        with nc.allow_non_contiguous_dma(reason="debug"):
                          sync.dma_start(out=out[640:768, 0:1], in_=nm1[:]).then_inc(s_st, 16)
                          sync.dma_start(out=out[640:768, 1:2], in_=nm2[:]).then_inc(s_st, 16)
                          sync.dma_start(out=out[640:768, 2:3], in_=ss1[:]).then_inc(s_st, 16)
                          sync.dma_start(out=out[640:768, 3:4], in_=ss2[:]).then_inc(s_st, 16)
                          sync.dma_start(out=out[640:768, 4:5], in_=uacc[:]).then_inc(s_st, 16)
                          sync.dma_start(out=out[640:768, 5:6], in_=recd[:]).then_inc(s_st, 16)
                          sync.dma_start(out=out[640:768, 6:7], in_=betan[:]).then_inc(s_st, 16)
                          sync.wait_ge(s_st, st0a + 16 * 13)
                  tot["st"] += 16 * 13

        qkctx.close()
        if stage == "att":
            dbg_out_bf16("dbg_att", [av[:, hc, :] for hc in range(H)])

        if lvl >= 3:
          with ExitStack() as octx:
              ao = sbuf("ao", [128, DC, TQ], F32, octx)
              wob = [sbuf(f"wob{i}", [128, DC, 128], BF16, octx) for i in range(2)]
              wof = sbuf("wof", [128, DC, 128], F32, octx)
              p6ps = ExitStack()
              ops = [psum(f"wo_ps{i}", [128, 512], F32, p6ps) for i in range(2)]
              sr2x = psum("sr2x", [1, TQ], F32, p6ps)
              sr2xx = psum("sr2xx", [1, TQ], F32, p6ps)
              aob = [sbuf(f"aob{i}", [128, 512], BF16, octx) for i in range(2)]
              aoq = [sbuf(f"aoq{i}", [128, 512], BF16, octx) for i in range(2)]

              wo_r = wo.rearrange("(hc p) m -> p hc m", p=128)
              do_stats = (stage != "wo0")
              do_stat_mm = (stage not in ("wo0", "wo1"))
              (ldg0, a0, b0, c0, d0, g0o, ld20o) = (tot["ldg"], tot["a"], tot["b"],
                                                    tot["c"], tot["d"], tot["g"], tot["ld2"])
              with nc.Block("p6") as blk:
                  @blk.sync
                  def _(sync):
                      for dcol in range(DC):
                          if dcol >= 1:
                              sync.wait_ge(s_g, g0o + dcol)
                          sync.dma_start(out=wof[:],
                                         in_=wo_r[:, :, 128 * dcol:128 * (dcol + 1)]).then_inc(s_ld2, 16)

                  @blk.gpsimd
                  def _(g):
                      for dcol in range(DC):
                          g.wait_ge(s_ld2, ld20o + 16 * (dcol + 1))
                          if dcol >= 2:
                              g.wait_ge(s_b, b0 + 2 * (dcol - 1))
                          g.tensor_copy(wob[dcol % 2][:], wof[:]).then_inc(s_g, 1)

                  @blk.tensor
                  def _(t):
                      it = 0
                      for dcol in range(DC):
                          t.wait_ge(s_g, g0o + dcol + 1)
                          for th in range(2):
                              if it >= 2:
                                  if do_stats:
                                      t.wait_ge(s_a, a0 + it - 1)
                                  t.wait_ge(s_d, d0 + it - 1)
                              for hc in range(DC):
                                  mm = t.matmul(ops[it % 2][:], wob[dcol % 2][:, hc, :],
                                                av[:, hc, 512 * th:512 * (th + 1)],
                                                start=(hc == 0), stop=(hc == DC - 1))
                              mm.then_inc(s_b, 1)
                              if do_stat_mm and it >= 1:
                                  pit = it - 1
                                  pd, pt = divmod(pit, 2)
                                  t.wait_ge(s_a, a0 + pit + 1)
                                  t.matmul(sr2x[0:1, 512 * pt:512 * (pt + 1)], ones1[:],
                                           aob[pit % 2][:],
                                           start=(pd == 0), stop=(pd == DC - 1))
                                  t.matmul(sr2xx[0:1, 512 * pt:512 * (pt + 1)], ones1[:],
                                           aoq[pit % 2][:],
                                           start=(pd == 0), stop=(pd == DC - 1)).then_inc(s_c, 1)
                              it += 1
                      if not do_stat_mm:
                          return
                      pit = 2 * DC - 1
                      pd, pt = divmod(pit, 2)
                      t.wait_ge(s_a, a0 + pit + 1)
                      t.matmul(sr2x[0:1, 512 * pt:512 * (pt + 1)], ones1[:], aob[pit % 2][:],
                               start=(pd == 0), stop=(pd == DC - 1))
                      t.matmul(sr2xx[0:1, 512 * pt:512 * (pt + 1)], ones1[:], aoq[pit % 2][:],
                               start=(pd == 0), stop=(pd == DC - 1)).then_inc(s_c, 1)

                  @blk.scalar
                  def _(s):
                      if not do_stats:
                          return
                      for it in range(2 * DC):
                          dcol, th = divmod(it, 2)
                          s.wait_ge(s_d, d0 + it + 1)
                          if it >= 2 and do_stat_mm:
                              s.wait_ge(s_c, c0 + it - 1)
                          s.copy(aob[it % 2][:], ao[:, dcol, 512 * th:512 * (th + 1)])
                          s.square(aoq[it % 2][:],
                                   ao[:, dcol, 512 * th:512 * (th + 1)]).then_inc(s_a, 1)

                  @blk.vector
                  def _(v):
                      for it in range(2 * DC):
                          dcol, th = divmod(it, 2)
                          v.wait_ge(s_b, b0 + it + 1)
                          v.tensor_copy(ao[:, dcol, 512 * th:512 * (th + 1)],
                                        ops[it % 2][:]).then_inc(s_d, 1)
              tot["ldg"] += 16 * DC
              tot["d"] += 2 * DC
              tot["b"] += 2 * DC
              tot["a"] += 2 * DC
              tot["c"] += 2 * DC

              if stage in ("wo", "wo0", "wo1"):
                  st0w = tot["st"]
                  with nc.Block("dbg_wo") as blk:
                      @blk.sync
                      def _(sync):
                          for dc in range(DC):
                              sync.dma_start(out=out[128 * dc:128 * (dc + 1), :],
                                             in_=ao[:, dc, :]).then_inc(s_st, 16)
                          sync.wait_ge(s_st, st0w + 16 * DC)
                  tot["st"] += 16 * DC

              ma2 = sbuf("ma2", [1, TQ], F32, octx)
              a2row = sbuf("a2row", [1, TQ], F32, octx)
              b2row = sbuf("b2row", [1, TQ], F32, octx)
              a2_s = sbuf("a2_s", [128, 1024], F32, octx)
              b2_s = sbuf("b2_s", [128, 1024], F32, octx)
              if stage not in ("wo0", "wo1"):
                  emit_ln_rows("p7", sr2x, sr2xx, ma2, a2row, b2row, D)
              p6ps.close()
              if stage == "ln2":
                  st0w = tot["st"]
                  with nc.Block("dbg_ln2") as blk:
                      @blk.sync
                      def _(sync):
                        with nc.allow_non_contiguous_dma(reason="debug"):
                          sync.dma_start(out=out[0:1, :], in_=a2row[0:1, :]).then_inc(s_st, 16)
                          sync.dma_start(out=out[1:2, :], in_=b2row[0:1, :]).then_inc(s_st, 16)
                          sync.dma_start(out=out[2:3, :], in_=ma2[0:1, :]).then_inc(s_st, 16)
                          sync.wait_ge(s_st, st0w + 48)
                  tot["st"] += 48
              ps_bc2 = psum("ps_bc2", [128, 1024], F32, octx)
              if stage not in ("wo", "wo0", "wo1", "ln2"):
                emit_bcast("p7b", a2row, b2row, a2_s, b2_s, ps_bc2, 1024)
              if stage not in ("wo", "wo0", "wo1", "ln2"):

                xo = [sbuf(f"p8_xo{i}", [128, TQ], F32, octx) for i in range(2)]
                t8 = [sbuf(f"p8_t{i}", [128, TQ], F32, octx) for i in range(2)]
                x2c = [sbuf(f"p8_x2{i}", [128, TQ], F32, octx) for i in range(2)]
                x2b = [sbuf(f"p8_xb{i}", [128, TQ], BF16, octx) for i in range(2)]
                x2q = [sbuf(f"p8_xq{i}", [128, TQ], BF16, octx) for i in range(2)]
                p8ps = ExitStack()
                sr3x = psum("sr3x", [1, TQ], F32, p8ps)
                sr3xx = psum("sr3xx", [1, TQ], F32, p8ps)
                ld0, a0, b0, c0, d0 = tot["ld"], tot["a"], tot["b"], tot["c"], tot["d"]
                st0 = tot["st"]
                with nc.Block("p8") as blk:
                    @blk.sync
                    def _(sync):
                        for dc in range(DC):
                            if dc >= 2:
                                sync.wait_ge(s_b, b0 + dc - 1)
                            sync.dma_start(out=xo[dc % 2][:],
                                           in_=xkv[128 * dc:128 * (dc + 1), 512:1536]).then_inc(s_ld, 16)

                    @blk.gpsimd
                    def _(g):
                        for dc in range(DC):
                            if dc >= 2:
                                g.wait_ge(s_b, b0 + dc - 1)
                            g.tensor_mul(t8[dc % 2][:], ao[:, dc, :], a2_s[:]).then_inc(s_a, 1)

                    @blk.vector
                    def _(v):
                        for dc in range(DC):
                            v.wait_ge(s_a, a0 + dc + 1)
                            v.wait_ge(s_ld, ld0 + 16 * (dc + 1))
                            if dc >= 2:
                                v.wait_ge(s_st, st0 + 16 * (dc - 1))
                                v.wait_ge(s_c, c0 + dc - 1)
                            v.tensor_sub(t8[dc % 2][:], t8[dc % 2][:], b2_s[:])
                            v.tensor_add(x2c[dc % 2][:], t8[dc % 2][:], xo[dc % 2][:]).then_inc(s_b, 1)

                    @blk.scalar
                    def _(s):
                        for dc in range(DC):
                            s.wait_ge(s_b, b0 + dc + 1)
                            if dc >= 2:
                                s.wait_ge(s_e, tot["e"] + dc - 1)
                            s.copy(x2b[dc % 2][:], x2c[dc % 2][:])
                            s.square(x2q[dc % 2][:], x2c[dc % 2][:]).then_inc(s_c, 1)
                            s.dma_start(out=x2scr[128 * dc:128 * (dc + 1), :],
                                        in_=x2c[dc % 2][:]).then_inc(s_st, 16)

                    @blk.tensor
                    def _(t):
                        for dc in range(DC):
                            t.wait_ge(s_c, c0 + dc + 1)
                            for th in range(2):
                                mm = t.matmul(sr3x[0:1, 512 * th:512 * (th + 1)], ones1[:],
                                              x2b[dc % 2][:, 512 * th:512 * (th + 1)],
                                              start=(dc == 0), stop=(dc == DC - 1))
                                mm = t.matmul(sr3xx[0:1, 512 * th:512 * (th + 1)], ones1[:],
                                              x2q[dc % 2][:, 512 * th:512 * (th + 1)],
                                              start=(dc == 0), stop=(dc == DC - 1))
                            mm.then_inc(s_e, 1)
                tot["ld"] += 16 * DC
                tot["a"] += DC
                tot["b"] += DC
                tot["c"] += DC
                tot["e"] += DC
                tot["st"] += 16 * DC


                emit_ln_rows("p9", sr3x, sr3xx, ma2, a2row, b2row, D)
                p8ps.close()
                emit_bcast("p9b", a2row, b2row, a2_s, b2_s, ps_bc2, 1024)

                ld0, a0, b0 = tot["ld"], tot["a"], tot["b"]
                st_spill = tot["st"]
                with nc.Block("p9_app") as blk:
                    @blk.sync
                    def _(sync):
                        sync.wait_ge(s_st, st_spill)
                        for dc in range(DC):
                            if dc >= 2:
                                sync.wait_ge(s_a, a0 + dc - 1)
                            sync.dma_start(out=xo[dc % 2][:],
                                           in_=x2scr[128 * dc:128 * (dc + 1), :]).then_inc(s_ld, 16)

                    @blk.gpsimd
                    def _(g):
                        for dc in range(DC):
                            g.wait_ge(s_ld, ld0 + 16 * (dc + 1))
                            if dc >= 2:
                                g.wait_ge(s_b, b0 + dc - 1)
                            g.tensor_mul(t8[dc % 2][:], xo[dc % 2][:], a2_s[:]).then_inc(s_a, 1)

                    @blk.vector
                    def _(v):
                        for dc in range(DC):
                            v.wait_ge(s_a, a0 + dc + 1)
                            v.tensor_sub(h2[:, dc, :], t8[dc % 2][:], b2_s[:]).then_inc(s_b, 1)
                tot["ld"] += 16 * DC
                tot["a"] += DC
                tot["b"] += DC

                if stage == "x1":
                    st0 = tot["st"]
                    ldx = tot["ld"]
                    with nc.Block("pout_x1") as blk:
                        @blk.sync
                        def _(sync):
                            for dc in range(DC):
                                if dc >= 2:
                                    sync.wait_ge(s_st, st0 + 16 * (dc - 1))
                                sync.dma_start(out=xo[dc % 2][:],
                                               in_=x2scr[128 * dc:128 * (dc + 1), :]).then_inc(s_ld, 16)
                                sync.wait_ge(s_ld, ldx + 16 * (dc + 1))
                                sync.dma_start(out=out[128 * dc:128 * (dc + 1), :],
                                               in_=xo[dc % 2][:]).then_inc(s_st, 16)
                            sync.wait_ge(s_st, st0 + 16 * DC)
                    tot["ld"] += 16 * DC
                    tot["st"] += 16 * DC
    if lvl <= 3:
        root.close()
        nc.finalize()
        return nc

    with ExitStack() as fctx:
        dacc = sbuf("dacc", [128, DC, TQ], F32, fctx)
        p10ps = ExitStack()
        hid = sbuf("hid", [128, 8, TQ], BF16, p10ps)
        wut = [sbuf(f"wut{i}", [128, DC, 128], BF16, p10ps) for i in range(2)]
        wgt = [sbuf(f"wgt{i}", [128, DC, 128], BF16, p10ps) for i in range(2)]
        wdt = sbuf("wdt", [128, 8, DC, 128], BF16, p10ps)
        ups = psum("up_ps", [128, TQ], F32, p10ps)
        gps = psum("g_ps", [128, TQ], F32, p10ps)
        dps = [psum(f"d_ps{i}", [128, TQ], F32, p10ps) for i in range(2)]
        sil = sbuf("sil", [128, TQ], BF16, p10ps)

        g_upT_r = g_upT.rearrange("(r p) i (dc j) -> p r i dc j", r=8, p=128, dc=DC, j=128)
        g_gT_r = g_gT.rearrange("(r p) i (dc j) -> p r i dc j", r=8, p=128, dc=DC, j=128)
        g_dT_r = g_dT.rearrange("(r p) (dc seg) (k q) -> p r dc seg k q",
                                r=8, p=128, dc=2, seg=4, k=16, q=128)

        NB = 8
        ld0, ld20 = tot["ld"], tot["ld2"]
        a0, b0, c0, d0, e0 = tot["a"], tot["b"], tot["c"], tot["d"], tot["e"]
        with nc.Block("p10") as blk:
            @blk.sync
            def _(sync):
                sync.wait_ge(s_cc, cc_gathers_done)
                for ec in range(EC):
                    if ec >= 2:
                        sync.wait_ge(s_b, b0 + ec - 1)
                    r, el = divmod(ec, 8)
                    sync.dma_start(out=wut[ec % 2][:],
                                   in_=g_upT_r[:, r, el, :, :]).then_inc(s_ld, 16)
                    sync.dma_start(out=wgt[ec % 2][:],
                                   in_=g_gT_r[:, r, el, :, :]).then_inc(s_ld, 16)

            @blk.scalar
            def _(s):
                s.wait_ge(s_cc, cc_gathers_done)
                for eb in range(NB):
                    for ei in range(8):
                        ec = 8 * eb + ei
                        if eb >= 1:
                            s.wait_ge(s_d, d0 + eb * DC)
                        seg, kk = divmod(ec, 16)
                        for dh in range(2):
                            s.dma_start(
                                out=wdt[:, ei, :, :].rearrange("p (r dh) q -> p dh r q", r=8, dh=2)[:, dh],
                                in_=g_dT_r[:, :, dh, seg, kk, :]
                            ).then_inc(s_ld2, 16)
                    for ei in range(8):
                        ec = 8 * eb + ei
                        s.wait_ge(s_b, b0 + ec + 1)
                        if ec >= 1:
                            s.wait_ge(s_c, c0 + ec)
                        s.activation(sil[:], ups[:], AF.Silu, scale=h_bc[:, 0:1]).then_inc(s_a, 1)

            @blk.tensor
            def _(t):
                for eb in range(NB):
                    for ei in range(8):
                        ec = 8 * eb + ei
                        t.wait_ge(s_ld, ld0 + 32 * (ec + 1))
                        if ec >= 1:
                            t.wait_ge(s_c, c0 + ec)
                        for dc in range(DC):
                            for th in range(2):
                                t.matmul(ups[:, 512 * th:512 * (th + 1)],
                                         wut[ec % 2][:, dc, :],
                                         h2[:, dc, 512 * th:512 * (th + 1)],
                                         start=(dc == 0), stop=(dc == DC - 1))
                        for dc in range(DC):
                            for th in range(2):
                                mm = t.matmul(gps[:, 512 * th:512 * (th + 1)],
                                              wgt[ec % 2][:, dc, :],
                                              h2[:, dc, 512 * th:512 * (th + 1)],
                                              start=(dc == 0), stop=(dc == DC - 1))
                        mm.then_inc(s_b, 1)
                    t.wait_ge(s_ld2, ld20 + 32 * 8 * (eb + 1))
                    t.wait_ge(s_c, c0 + 8 * (eb + 1))
                    for dc in range(DC):
                        di = eb * DC + dc
                        if di >= 2:
                            t.wait_ge(s_e, e0 + di - 1)
                        for ei in range(8):
                            for th in range(2):
                                mm = t.matmul(dps[di % 2][:, 512 * th:512 * (th + 1)],
                                              wdt[:, ei, dc, :],
                                              hid[:, ei, 512 * th:512 * (th + 1)],
                                              start=(ei == 0), stop=(ei == 7))
                                if ei == 0 and th == 0 and dc > 0:
                                    mm.then_inc(s_d, 1)
                    t.matmul(ups[0:1, 0:1], ones1[:], ones1[:],
                             start=False, stop=True, skip_group_check=True).then_inc(s_d, 1)

            @blk.vector
            def _(v):
                di = 0
                for eb in range(NB):
                    for ei in range(8):
                        ec = 8 * eb + ei
                        v.wait_ge(s_a, a0 + ec + 1)
                        v.scalar_tensor_tensor(hid[:, ei, :], gps[:], h_bc[:, 1:2], sil[:],
                                               op0=ALU.mult, op1=ALU.mult).then_inc(s_c, 1)
                    for dc in range(DC):
                        v.wait_ge(s_d, d0 + di + 1)
                        if eb == 0:
                            v.tensor_copy(dacc[:, dc, :], dps[di % 2][:]).then_inc(s_e, 1)
                        else:
                            v.tensor_add(dacc[:, dc, :], dacc[:, dc, :],
                                         dps[di % 2][:]).then_inc(s_e, 1)
                        di += 1
        tot["ld"] += 32 * EC
        tot["ld2"] += 32 * EC
        tot["a"] += EC
        tot["b"] += EC
        tot["c"] += EC
        tot["d"] += NB * DC
        tot["e"] += NB * DC
        p10ps.close()

        with ExitStack() as lctx:
            f4b = [sbuf(f"f4b{i}", [128, TQ], BF16, lctx) for i in range(2)]
            f4q = [sbuf(f"f4q{i}", [128, TQ], BF16, lctx) for i in range(2)]
            sr4x = psum("sr4x", [1, TQ], F32, lctx)
            sr4xx = psum("sr4xx", [1, TQ], F32, lctx)
            ma4 = sbuf("ma4", [1, TQ], F32, lctx)
            a4row = sbuf("a4row", [1, TQ], F32, lctx)
            b4row = sbuf("b4row", [1, TQ], F32, lctx)
            ps_bc4 = psum("ps_bc4", [128, 1024], F32, lctx)
            a4_s = sbuf("a4_s", [128, 1024], F32, lctx)
            b4_s = sbuf("b4_s", [128, 1024], F32, lctx)
            xo2 = [sbuf(f"xo2{i}", [128, TQ], F32, lctx) for i in range(2)]
            t12 = [sbuf(f"t12{i}", [128, TQ], F32, lctx) for i in range(2)]
            ob = [sbuf(f"ob{i}", [128, TQ], F32, lctx) for i in range(2)]

            a0, b0 = tot["a"], tot["b"]
            with nc.Block("p11_stats") as blk:
                @blk.scalar
                def _(s):
                    for dc in range(DC):
                        if dc >= 2:
                            s.wait_ge(s_b, b0 + dc - 1)
                        s.copy(f4b[dc % 2][:], dacc[:, dc, :])
                        s.square(f4q[dc % 2][:], dacc[:, dc, :]).then_inc(s_a, 1)

                @blk.tensor
                def _(t):
                    for dc in range(DC):
                        t.wait_ge(s_a, a0 + dc + 1)
                        for th in range(2):
                            mm = t.matmul(sr4x[0:1, 512 * th:512 * (th + 1)], ones1[:],
                                          f4b[dc % 2][:, 512 * th:512 * (th + 1)],
                                          start=(dc == 0), stop=(dc == DC - 1))
                            mm = t.matmul(sr4xx[0:1, 512 * th:512 * (th + 1)], ones1[:],
                                          f4q[dc % 2][:, 512 * th:512 * (th + 1)],
                                          start=(dc == 0), stop=(dc == DC - 1))
                        mm.then_inc(s_b, 1)
            tot["a"] += DC
            tot["b"] += DC

            emit_ln_rows("p11", sr4x, sr4xx, ma4, a4row, b4row, D,
                         c_ap=c_row[0:1, 0:1], c2_ap=c_row[0:1, 1:2])
            st0d = tot["st"]
            with nc.Block("dbg_p11") as blk:
                @blk.sync
                def _(sync):
                    sync.dma_start(out=dbg[0:1, 0:4], in_=hrow[:]).then_inc(s_st, 16)
                    sync.dma_start(out=dbg[1:2, 0:2], in_=c_row[:]).then_inc(s_st, 16)
                    sync.dma_start(out=dbg[2:3, :], in_=a4row[0:1, :]).then_inc(s_st, 16)
                    sync.dma_start(out=dbg[3:4, :], in_=b4row[0:1, :]).then_inc(s_st, 16)
                    sync.dma_start(out=dbg[4:5, :], in_=ma4[0:1, :]).then_inc(s_st, 16)
                    sync.dma_start(out=dbg[5:6, 0:4], in_=partials[:]).then_inc(s_st, 16)
                    sync.dma_start(out=dbg[6:7, :], in_=x2scr[0:1, :]).then_inc(s_st, 16)
                    sync.dma_start(out=dbg[7:8, :], in_=dacc[0:1, 0, :]).then_inc(s_st, 16)
                    sync.wait_ge(s_st, st0d + 128)
            tot["st"] += 128
            with nc.Block("dbg_p11b") as blk:
                @blk.gpsimd
                def _(g):
                    g.dma_start(out=dbg[8:9, :], in_=h2[0:1, 0, :]).then_inc(s_st, 16)
                    g.dma_start(out=dbg[9:10, :], in_=g_upT[0:1, 0, 0:1024]).then_inc(s_st, 16)
                    g.dma_start(out=dbg[10:11, :], in_=g_dT[0:1, 0, 0:1024]).then_inc(s_st, 16)
                    g.wait_ge(s_st, tot["st"] + 48)
            tot["st"] += 48
            emit_bcast("p11b", a4row, b4row, a4_s, b4_s, ps_bc4, 1024)

            ld0, a0, b0 = tot["ld"], tot["a"], tot["b"]
            st0 = tot["st"]
            with nc.Block("p12") as blk:
                @blk.sync
                def _(sync):
                    for dc in range(DC):
                        if dc >= 2:
                            sync.wait_ge(s_b, b0 + dc - 1)
                        sync.dma_start(out=xo2[dc % 2][:],
                                       in_=x2scr[128 * dc:128 * (dc + 1), :]).then_inc(s_ld, 16)

                @blk.gpsimd
                def _(g):
                    for dc in range(DC):
                        if dc >= 2:
                            g.wait_ge(s_b, b0 + dc - 1)
                        g.tensor_mul(t12[dc % 2][:], dacc[:, dc, :], a4_s[:]).then_inc(s_a, 1)

                @blk.vector
                def _(v):
                    for dc in range(DC):
                        v.wait_ge(s_a, a0 + dc + 1)
                        v.wait_ge(s_ld, ld0 + 16 * (dc + 1))
                        if dc >= 2:
                            v.wait_ge(s_st, st0 + 16 * (dc - 1))
                        v.tensor_sub(t12[dc % 2][:], t12[dc % 2][:], b4_s[:])
                        v.tensor_add(ob[dc % 2][:], t12[dc % 2][:], xo2[dc % 2][:]).then_inc(s_b, 1)

                @blk.scalar
                def _(s):
                    for dc in range(DC):
                        s.wait_ge(s_b, b0 + dc + 1)
                        s.dma_start(out=out[128 * dc:128 * (dc + 1), :],
                                    in_=ob[dc % 2][:]).then_inc(s_st, 16)
                    s.wait_ge(s_st, st0 + 16 * DC)
            tot["ld"] += 16 * DC
            tot["a"] += DC
            tot["b"] += DC
            tot["st"] += 16 * DC

    root.close()
    nc.finalize()
    return nc



def _host_inputs(inputs):
    x = np.asarray(inputs["L0_attn_x"], dtype=np.float32)
    Wq = np.ascontiguousarray(np.asarray(inputs["L0_attn_Wq"], dtype=np.float32))
    Wk = np.ascontiguousarray(np.asarray(inputs["L0_attn_Wk"], dtype=np.float32))
    Wv = np.ascontiguousarray(np.asarray(inputs["L0_attn_Wv"], dtype=np.float32))
    Wo = np.ascontiguousarray(np.asarray(inputs["L0_attn_Wo"], dtype=np.float32))
    lam = np.asarray(inputs["L0_attn_lam"], dtype=np.float32)
    Wup = np.asarray(inputs["L0_ffn_W_up"], dtype=np.float32)
    Wg = np.asarray(inputs["L0_ffn_W_gate"], dtype=np.float32)
    Wd = np.asarray(inputs["L0_ffn_W_down"], dtype=np.float32)

    ident = np.eye(128, dtype=np.float32)
    lamb = np.ascontiguousarray(np.broadcast_to(lam[None, :], (128, H)))
    invf = 10000.0 ** (-(np.arange(SUB, dtype=np.float64) / SUB))

    in_maps = []
    for c in range(8):
        b, half = divmod(c, 2)
        q0 = 1024 * half
        xkv = np.zeros((D, TK), dtype=np.float32)
        lo = q0 - 512
        src_lo = max(lo, 0)
        xkv[:, src_lo - lo:] = x[b, src_lo:q0 + 1024, :].T
        xkv = np.ascontiguousarray(xkv)

        i_loc = np.arange(128)[:, None]
        j_sl = np.arange(SLAB)[None, :]
        band = (j_sl - i_loc >= 1) & (j_sl - i_loc <= WIN)
        m = np.zeros((128, NT, SLAB), dtype=np.float32)
        for t in range(NT):
            ok = band & (128 * t + j_sl >= 512 - q0)
            m[:, t, :] = ok.astype(np.float32)
        masks = m.astype(ml_dtypes.bfloat16)

        pos_q = q0 + np.arange(TQ, dtype=np.float64)
        ang_q = pos_q[None, :] * invf[:, None]
        cosq = (np.cos(ang_q) / 8.0).astype(np.float32)
        sinq = (np.sin(ang_q) / 8.0).astype(np.float32)
        pos_k = q0 - 512 + np.arange(TK, dtype=np.float64)
        ang_k = pos_k[None, :] * invf[:, None]
        cosk = np.cos(ang_k).astype(np.float32)
        sink = np.sin(ang_k).astype(np.float32)

        in_maps.append({
            "xkv": xkv,
            "wq": Wq, "wk": Wk, "wv": Wv, "wo": Wo,
            "wup_s": np.ascontiguousarray(Wup[1024 * c:1024 * (c + 1)]),
            "wg_s": np.ascontiguousarray(Wg[1024 * c:1024 * (c + 1)]),
            "wd_s": np.ascontiguousarray(Wd[256 * c:256 * (c + 1)]),
            "masks": masks, "lamb": lamb, "ident": ident,
            "cosq": np.ascontiguousarray(cosq), "sinq": np.ascontiguousarray(sinq),
            "cosk": np.ascontiguousarray(cosk), "sink": np.ascontiguousarray(sink),
        })
    return in_maps


def kernel(**inputs):
    in_maps = _host_inputs(inputs)
    key = STAGE
    if key not in _nc_cache:
        _nc_cache[key] = build_nc(STAGE)
    nc = _nc_cache[key]
    res = run_bass_kernel_spmd(nc, in_maps, core_ids=list(range(8)))
    outp = np.empty((4, 2048, D), dtype=np.float32)
    for c in range(8):
        b, half = divmod(c, 2)
        o = res.results[c]["out"]
        bad = ~np.isfinite(o)
        if bad.any():
            rr, cc = np.where(bad)
            print(f"[DBGNAN] core {c}: {bad.sum()} nan; d-chunks {sorted(set(rr // 128))} "
                  f"tok range {cc.min()}-{cc.max()}", flush=True)
            d = res.results[c].get("dbg")
            if d is not None:
                print(f"[DBGNAN]   hrow {d[0, 0:4]} c_row {d[1, 0:2]} "
                      f"a4 finite={np.isfinite(d[2]).all()} ma4 finite={np.isfinite(d[4]).all()} "
                      f"a4[0:4]={d[2, 0:4]}", flush=True)
                print(f"[DBGNAN]   x2scr row0 finite={np.isfinite(d[6]).all()} "
                      f"nan_at={np.where(~np.isfinite(d[6]))[0][:5]} "
                      f"dacc row0 finite={np.isfinite(d[7]).all()} "
                      f"dnan_at={np.where(~np.isfinite(d[7]))[0][:5]}", flush=True)
        outp[b, 1024 * half:1024 * (half + 1), :] = o.T
    return outp
